# version 12
# baseline (speedup 1.0000x reference)
import os
from contextlib import ExitStack

import numpy as np

import concourse.bacc as bacc
import concourse.mybir as mybir
from concourse.bass_utils import run_bass_kernel_spmd

F32 = mybir.dt.float32
BF16 = mybir.dt.bfloat16
I32 = mybir.dt.int32
AF = mybir.ActivationFunctionType
ALU = mybir.AluOpType
AX = mybir.AxisListType

D = 1024
NS = 16
SL = 512
SEQ = 8192
SCALE = 0.125
QD = 1.0 / 127.0

ITERS = []
for _qi, _caps in ((12, 3), (13, 6), (14, 9), (15, 12)):
    for _ks in range(_caps):
        ITERS.append((_qi, _ks, 'g'))
    for _os in range(12, _qi):
        ITERS.append((_qi, _os, 'f'))
    ITERS.append((_qi, _qi, 'd'))

STAGE = int(os.environ.get("MK_STAGE", "3"))
DEV_TAPS = os.environ.get("MK_TAPS", "") == "1"



def _core_slots(g):
    owned = [g, g + 4, g + 8, g + 12]
    nonowned = [c for c in range(16) if c not in owned]
    return nonowned + owned, owned


def _selectors(perm):
    sel = np.zeros((1, 64), np.float32)
    for i, (qi, ks, kind) in enumerate(ITERS):
        sel[0, i] = 1.0 if (kind != 'g' or perm[ks] < perm[qi]) else 0.0
    return sel


def _host_inputs(L0_attn_hidden_states, L0_attn_c_attn_kernel, L0_attn_c_attn_bias,
                 L0_attn_c_proj_kernel, L0_attn_c_proj_bias,
                 L0_ffn_w_gate, L0_ffn_w_up, L0_ffn_w_down):
    x = np.asarray(L0_attn_hidden_states)
    wqkv = np.asarray(L0_attn_c_attn_kernel)
    battn = np.ascontiguousarray(np.asarray(L0_attn_c_attn_bias, np.float32))
    wproj = np.asarray(L0_attn_c_proj_kernel)
    bproj = np.ascontiguousarray(np.asarray(L0_attn_c_proj_bias, np.float32))

    def tile_lhsT(w_T, n_out, n_in):
        t = w_T.reshape(n_in, 128, n_out, 128)
        return np.ascontiguousarray(t.transpose(2, 0, 1, 3))

    wg_t = tile_lhsT(np.asarray(L0_ffn_w_gate).T, 24, 8)
    wu_t = tile_lhsT(np.asarray(L0_ffn_w_up).T, 24, 8)
    wd_t = tile_lhsT(np.asarray(L0_ffn_w_down).T, 8, 24)
    wqkvT = wqkv.T
    wqk_t = tile_lhsT(np.ascontiguousarray(wqkvT[:, :2048]), 16, 8)
    wv = np.ascontiguousarray(wqkvT[:, 2048:])
    wv_t = np.ascontiguousarray(wv.reshape(8, 128, 2, 512).transpose(2, 0, 1, 3))
    wp_t = tile_lhsT(np.ascontiguousarray(wproj.T), 8, 8)

    cores = []
    for b in range(2):
        for g in range(4):
            perm, owned = _core_slots(g)
            idx = np.concatenate([np.arange(c * SL, (c + 1) * SL) for c in perm])
            xp = np.ascontiguousarray(x[b][idx].T)
            cores.append({
                "xT": xp, "sel": _selectors(perm),
                "wg_t": wg_t, "wu_t": wu_t, "wd_t": wd_t,
                "wqk_t": wqk_t, "wv_t": wv_t, "wp_t": wp_t,
                "battn": battn, "bproj": bproj,
            })
    return cores


def _host_assemble(outs):
    final = np.empty((2, SEQ, D), np.float32)
    for b in range(2):
        for g in range(4):
            perm, owned = _core_slots(g)
            o = np.asarray(outs[b * 4 + g]["outT"], np.float32)
            for i, c in enumerate(owned):
                final[b, c * SL:(c + 1) * SL, :] = o[:, i * SL:(i + 1) * SL].T
    return final



CONF = {
    "xbf": {"qs"}, "qs": {"xbf"},
    "xsq": {"ksq", "pt0", "pt1", "oall"},
    "ksq": {"xsq", "pt0", "pt1"},
    "pt0": {"xsq", "ksq"}, "pt1": {"xsq", "ksq"},
    "oall": {"xsq"},
    "p1x": {"cvf", "bvf"}, "cvf": {"p1x", "bvf"}, "bvf": {"p1x", "cvf"},
    "wp": {"wq0", "wq1"}, "wq0": {"wp"}, "wq1": {"wp"},
    "erow": {"rrow"}, "rrow": {"erow"},
    "rrw": {"abr"}, "abr": {"rrw"},
    "f3bf": {"xh0_0", "xh0_1", "xh1_0", "xh1_1", "xh2_0", "xh2_1", "xh3_0", "xh3_1"},
    "xh0_0": {"f3bf"}, "xh0_1": {"f3bf"}, "xh1_0": {"f3bf"}, "xh1_1": {"f3bf"},
    "xh2_0": {"f3bf"}, "xh2_1": {"f3bf"}, "xh3_0": {"f3bf"}, "xh3_1": {"f3bf"},
    "f3h": {"xh4_0", "xh4_1", "xh5_0", "xh5_1", "xh6_0", "xh6_1", "xh7_0", "xh7_1"},
    "xh4_0": {"f3h"}, "xh4_1": {"f3h"}, "xh5_0": {"f3h"}, "xh5_1": {"f3h"},
    "xh6_0": {"f3h"}, "xh6_1": {"f3h"}, "xh7_0": {"f3h"}, "xh7_1": {"f3h"},
    "ht2": {"ht", "wv2"}, "f3sq": {"ht"},
    "rbs": {"kev", "sg"},
    "brb": {"bbr"}, "bbr": {"brb"},
    "p1xm": {"xms", "xaa", "cvi"}, "xms": {"p1xm"}, "xaa": {"p1xm"},
    "cvi": {"p1xm"},
    "ht": {"kb0", "kb1", "vb0", "vb1", "wv2", "ht2", "f3sq"},
    "kb0": {"ht", "wv2"}, "kb1": {"ht", "wv2"},
    "vb0": {"ht", "wv2"}, "vb1": {"ht", "wv2"},
    "wv2": {"ht", "kb0", "kb1", "vb0", "vb1", "ht2"},
    "sg": {"kev"}, "kev": {"sg"},
    "cvo0": {"xsq", "ksq", "pt0", "pt1", "oall", "cvo1"},
    "cvo1": {"xsq", "ksq", "pt0", "pt1", "oall", "cvo0"},
}


class Prog:
    ENGS = ("sync", "tensor", "vector", "scalar", "gpsimd")

    def __init__(self):
        self.plan = {e: [] for e in self.ENGS}
        self.cnt = {e: 0 for e in self.ENGS}
        self.last_w = {}
        self.readers = {}
        self.waited = {e: {} for e in self.ENGS}

    def _names(self, t):
        return (t,) + tuple(CONF.get(t, ()))

    def op(self, eng, emit, reads=(), writes=(), inc=1, pe_chain=False):
        needs = {}

        def need(w):
            if w:
                needs[w[0]] = max(needs.get(w[0], 0), w[1])

        for t in reads:
            need(self.last_w.get(t))
            for o in CONF.get(t, ()):
                need(self.last_w.get(o))
        for t in writes:
            for n in self._names(t):
                need(self.last_w.get(n))
                for r in self.readers.get(n, ()):
                    need(r)
        waits = []
        for src, val in needs.items():
            if src == eng and (pe_chain or eng == "tensor"):
                continue
            if self.waited[eng].get(src, 0) >= val:
                continue
            waits.append((src, val))
            self.waited[eng][src] = val
        self.cnt[eng] += inc
        my = (eng, self.cnt[eng])
        for t in writes:
            self.last_w[t] = my
            self.readers[t] = []
        for t in reads:
            self.readers.setdefault(t, []).append(my)
        self.plan[eng].append((waits, emit, inc))



def build_nc():
    nc = bacc.Bacc("TRN2", target_bir_lowering=False)

    xT = nc.dram_tensor("xT", [D, SEQ], F32, kind="ExternalInput")
    selT = nc.dram_tensor("sel", [1, 64], F32, kind="ExternalInput")
    wg_t = nc.dram_tensor("wg_t", [24, 8, 128, 128], I32, kind="ExternalInput")
    wu_t = nc.dram_tensor("wu_t", [24, 8, 128, 128], I32, kind="ExternalInput")
    wd_t = nc.dram_tensor("wd_t", [8, 24, 128, 128], I32, kind="ExternalInput")
    wqk_t = nc.dram_tensor("wqk_t", [16, 8, 128, 128], F32, kind="ExternalInput")
    wv_t = nc.dram_tensor("wv_t", [2, 8, 128, 512], F32, kind="ExternalInput")
    wp_t = nc.dram_tensor("wp_t", [8, 8, 128, 128], F32, kind="ExternalInput")
    battn = nc.dram_tensor("battn", [3072], F32, kind="ExternalInput")
    bprojT = nc.dram_tensor("bproj", [D], F32, kind="ExternalInput")
    outT = nc.dram_tensor("outT", [D, 4 * SL], F32, kind="ExternalOutput")
    taps = {}
    if DEV_TAPS:
        taps["t_xmid"] = nc.dram_tensor("t_xmid", [D, 1024], F32, kind="ExternalOutput")
        taps["t_xa"] = nc.dram_tensor("t_xa", [8, 128, 4 * SL], F32, kind="ExternalOutput")
        taps["t_k"] = nc.dram_tensor("t_k", [8, 128, SL], BF16, kind="ExternalOutput")
        taps["t_v"] = nc.dram_tensor("t_v", [4, 128, 1040], BF16, kind="ExternalOutput")
        taps["t_q"] = nc.dram_tensor("t_q", [8, 128, SL], BF16, kind="ExternalOutput")
        taps["t_c"] = nc.dram_tensor("t_c", [128, 64], F32, kind="ExternalOutput")
        taps["t_o"] = nc.dram_tensor("t_o", [128, 512], F32, kind="ExternalOutput")

    g_bf = nc.dram_tensor("g_bf", [24, 8, 128, 128], BF16, kind="Internal")
    u_bf = nc.dram_tensor("u_bf", [24, 8, 128, 128], BF16, kind="Internal")
    d_bf = nc.dram_tensor("d_bf", [8, 24, 128, 128], BF16, kind="Internal")
    qk_bf = nc.dram_tensor("qk_bf", [16, 8, 128, 128], BF16, kind="Internal")
    v_bf = nc.dram_tensor("v_bf", [2, 8, 128, 512], BF16, kind="Internal")
    p_bf = nc.dram_tensor("p_bf", [8, 8, 128, 128], BF16, kind="Internal")
    kK = nc.dram_tensor("kK", [NS, 8, 128, SL], BF16, kind="Internal")
    kV = nc.dram_tensor("kV", [NS, 4, 128, 1040], BF16, kind="Internal")
    qQ = nc.dram_tensor("qQ", [4, 8, 128, SL], BF16, kind="Internal")
    xm_d = nc.dram_tensor("xm_d", [8, 128, 4 * SL], F32, kind="Internal")
    xa_d = nc.dram_tensor("xa_d", [8, 128, 4 * SL], F32, kind="Internal")

    P = Prog()

    with ExitStack() as ctx:
        sbt = lambda n, s, d: ctx.enter_context(nc.sbuf_tensor(n, s, d))
        pst = lambda n: ctx.enter_context(nc.psum_tensor(n, [128, 1024], F32))

        XC = sbt("XC", [128, 8192], F32)
        XM = sbt("XM", [128, 8192], F32)
        HT = sbt("HT", [128, 24832], BF16)
        XB = sbt("XB", [128, 8192], BF16)
        XH = sbt("XH", [128, 8192], BF16)
        XQ = sbt("XQ", [128, 8192], BF16)
        CVO = XQ
        WV2 = HT
        WG = sbt("WG", [128, 2048], BF16)
        WU = sbt("WU", [128, 2048], BF16)
        WD2 = sbt("WD2", [128, 6144], BF16)
        WQ = sbt("WQ", [128, 2048], BF16)
        WP = WQ
        KEV = sbt("KEV", [128, 1024], BF16)
        RBS = KEV
        VSTG = sbt("VSTG", [128, 1040], BF16)
        TZ = sbt("TZ", [128, 512], F32)
        ABC = sbt("ABC", [128, 512], BF16)
        BBC = sbt("BBC", [128, 512], BF16)
        MSK = sbt("MSK", [128, 2048], BF16)
        BIA = sbt("BIA", [128, 64], F32)
        BCOL = sbt("BCOL", [128, 32], F32)
        BPC = sbt("BPC", [128, 8], F32)
        ONE1 = sbt("ONE1", [1, 128], BF16)
        ONE128 = sbt("ONE128", [128, 1], BF16)
        BVR = sbt("BVR", [1, 1024], BF16)
        SELR = sbt("SELR", [1, 64], F32)
        MRow = sbt("MRow", [1, 512], F32)
        ERow = sbt("ERow", [1, 512], F32)
        TRow = sbt("TRow", [1, 512], F32)
        RRow = ERow
        ABR = sbt("ABR", [1, 512], BF16)
        BBR = sbt("BBR", [1, 512], BF16)
        NKM = sbt("NKM", [1, 4], F32)
        NQM = sbt("NQM", [1, 4], F32)
        TT1 = sbt("TT1", [1, 4], F32)
        CVA = sbt("CVA", [1, 4], F32)
        BROW = sbt("BROW", [1, 64], F32)
        BRB = BBR
        RRW = ABR

        PS = [pst(f"PS{i}") for i in range(4)]

        def bank(i):
            return PS[i // 2][:, (i % 2) * 512:(i % 2) * 512 + 512]

        def bank2(i):
            return PS[i // 2][:]

        def dma(qeng, out, in_, reads, writes):
            P.op(qeng, lambda e, o=out, i=in_: e.dma_start(out=o, in_=i),
                 reads=reads, writes=writes, inc=16)

        def mm(out, lhsT, rhs, start, stop, reads, writes):
            P.op("tensor",
                 lambda e, o=out, l=lhsT, r=rhs, s1=start, s2=stop:
                     e.matmul(out=o, lhsT=l, rhs=r, start=s1, stop=s2),
                 reads=reads, writes=writes, inc=1, pe_chain=True)

        def vop(emit, reads, writes):
            P.op("vector", emit, reads=reads, writes=writes, inc=1)

        def aop(emit, reads, writes):
            P.op("scalar", emit, reads=reads, writes=writes, inc=1)

        def gop(emit, reads, writes, inc=1):
            P.op("gpsimd", emit, reads=reads, writes=writes, inc=inc)

        def convert(src, dst, n_outer, n_a, n_c, is_int):
            n = n_a * n_c
            for i in range(n_outer):
                pp = i % 2
                if is_int:
                    stage = XM[:, pp * 4096: pp * 4096 + n].bitcast(I32)
                    stag = "cvi"
                else:
                    stage = XC[:, pp * 4096: pp * 4096 + n]
                    stag = "cvf"
                dma("sync", stage.rearrange("p (a c) -> p a c", a=n_a),
                    src.ap()[i].rearrange("a p c -> p a c"),
                    reads=(), writes=(stag + str(pp),))
                vop(lambda e, s=stage, n0=n, q=pp: e.tensor_copy(
                    out=CVO[:, q * 2048: q * 2048 + n0] if n0 <= 2048 else CVO[:, 0:n0],
                    in_=s),
                    reads=(stag + str(pp),), writes=(f"cvo{pp}" if n <= 2048 else "cvo0",))
                co = CVO[:, pp * 2048: pp * 2048 + n] if n <= 2048 else CVO[:, 0:n]
                dma("gpsimd", dst.ap()[i].rearrange("a p c -> p a c"),
                    co.rearrange("p (a c) -> p a c", a=n_a),
                    reads=((f"cvo{pp}" if n <= 2048 else "cvo0"),), writes=())

        convert(wg_t, g_bf, 24, 8, 128, True)
        convert(wu_t, u_bf, 24, 8, 128, True)
        convert(wd_t, d_bf, 8, 24, 128, True)
        convert(wqk_t, qk_bf, 16, 8, 128, False)
        convert(wv_t, v_bf, 2, 8, 512, False)
        convert(wp_t, p_bf, 8, 8, 128, False)

        for f in range(24):
            dma("sync", BCOL[:, f:f + 1],
                battn.ap()[f * 128:(f + 1) * 128].rearrange("(p o) -> p o", o=1),
                reads=(), writes=(f"bc{f}",))
        for dc in range(8):
            dma("sync", BPC[:, dc:dc + 1],
                bprojT.ap()[dc * 128:(dc + 1) * 128].rearrange("(p o) -> p o", o=1),
                reads=(), writes=(f"bp{dc}",))
        dma("sync", XC[0:1, 0:1024], battn.ap()[2048:3072].rearrange("(o c) -> o c", o=1),
            reads=(), writes=("bvf",))
        vop(lambda e: e.tensor_copy(out=BVR[:], in_=XC[0:1, 0:1024]),
            reads=("bvf",), writes=("bvr",))
        dma("sync", SELR[:], selT.ap(), reads=(), writes=("selr",))

        gop(lambda e: e.memset(ONE1[:], 1.0), reads=(), writes=("one1",))
        gop(lambda e: e.memset(CVA[0:1, 3:4], 1e-5), reads=(), writes=("epsb",))
        gop(lambda e: e.memset(ONE128[:], 1.0), reads=(), writes=("one128",))
        gop(lambda e: e.memset(NKM[:], 0.0), reads=(), writes=("nkm",))
        gop(lambda e: e.memset(NQM[:], 0.0), reads=(), writes=("nqm",))
        gop(lambda e: e.memset(VSTG[:], 0.0), reads=(), writes=("vstg",))
        gop(lambda e: e.memset(
            VSTG[:].rearrange("p (h c) -> p h c", c=65)[:, :, 64:65], 1.0),
            reads=(), writes=("vstg",))
        gop(lambda e: e.iota(TZ[:], [[1, 512]], base=0, channel_multiplier=-1,
                             allow_small_or_imprecise_dtypes=True),
            reads=(), writes=("tzf",))
        for r in range(4):
            vop(lambda e, rr=r: e.tensor_single_scalar(
                out=MSK[:, rr * 512:(rr + 1) * 512], in_=TZ[:],
                scalar=float(rr * 128), op=ALU.is_ge),
                reads=("tzf",), writes=("msk",))

        def layer_norm(xtag, src, n_tt, stride):
            vop(lambda e, n=n_tt: e.tensor_copy(
                out=XB[:, 0:8 * stride], in_=src[:, 0:8 * stride]),
                reads=(xtag,), writes=("xbf",))
            aop(lambda e, n=n_tt: e.activation(
                out=XQ[:, 0:8 * stride], in_=src[:, 0:8 * stride], func=AF.Square),
                reads=(xtag,), writes=("xsq",))
            for tt in range(n_tt):
                off = tt * 512
                for dc in range(8):
                    mm(bank(6)[0:1, :], ONE128[:], XB[:, dc * stride + off: dc * stride + off + 512],
                       dc == 0, dc == 7, reads=("one128", "xbf"), writes=("pb6",))
                    mm(bank(7)[0:1, :], ONE128[:], XQ[:, dc * stride + off: dc * stride + off + 512],
                       dc == 0, dc == 7, reads=("one128", "xsq"), writes=("pb7",))
                vop(lambda e: e.tensor_scalar_mul(out=MRow[:], in0=bank(6)[0:1, :],
                                                  scalar1=1.0 / 1024),
                    reads=("pb6",), writes=("mrow",))
                vop(lambda e: e.tensor_scalar_mul(out=ERow[:], in0=bank(7)[0:1, :],
                                                  scalar1=1.0 / 1024),
                    reads=("pb7",), writes=("erow",))
                vop(lambda e: e.tensor_mul(out=TRow[:], in0=MRow[:], in1=MRow[:]),
                    reads=("mrow",), writes=("trow",))
                vop(lambda e: e.tensor_sub(out=ERow[:], in0=ERow[:], in1=TRow[:]),
                    reads=("erow", "trow"), writes=("erow",))
                aop(lambda e: e.activation(out=TRow[:], in_=ERow[:], func=AF.Ln,
                                           bias=CVA[0:1, 3:4]),
                    reads=("erow", "epsb"), writes=("trow",))
                aop(lambda e: e.activation(out=RRow[:], in_=TRow[:], func=AF.Exp, scale=-0.5),
                    reads=("trow",), writes=("rrow",))
                vop(lambda e: e.tensor_copy(out=ABR[:], in_=RRow[:]),
                    reads=("rrow",), writes=("abr",))
                vop(lambda e: e.scalar_tensor_tensor(
                    out=BBR[:], in0=MRow[:], scalar=-1.0, in1=RRow[:],
                    op0=ALU.mult, op1=ALU.mult),
                    reads=("mrow", "rrow"), writes=("bbr",))
                mm(bank(6), ONE1[:], ABR[:], True, True, reads=("one1", "abr"), writes=("pb6",))
                vop(lambda e: e.tensor_copy(out=ABC[:], in_=bank(6)), reads=("pb6",), writes=("abc",))
                mm(bank(7), ONE1[:], BBR[:], True, True, reads=("one1", "bbr"), writes=("pb7",))
                vop(lambda e: e.tensor_copy(out=BBC[:], in_=bank(7)), reads=("pb7",), writes=("bbc",))
                for dc in range(8):
                    o = dc * stride + off
                    vop(lambda e, o0=o: e.tensor_mul(
                        out=TZ[:], in0=src[:, o0:o0 + 512], in1=ABC[:]),
                        reads=(xtag, "abc"), writes=("tzf",))
                    vop(lambda e, o0=o: e.tensor_add(
                        out=XH[:, o0:o0 + 512], in0=TZ[:], in1=BBC[:]),
                        reads=("tzf", "bbc"), writes=(f"xh{dc}_{tt}",))

        def ffn(xtag, res, dst, dtag, off, stride, wpar):
            for hch in range(24):
                pp = (hch + wpar) % 2
                gq = 0 if hch % 2 == 0 else 4
                uq = 2 if hch % 2 == 0 else 6
                dma("sync", WG[:, pp * 1024:(pp + 1) * 1024].rearrange("p (a c) -> p a c", a=8),
                    g_bf.ap()[hch].rearrange("a p c -> p a c"), reads=(), writes=(f"wg{pp}",))
                dma("sync", WU[:, pp * 1024:(pp + 1) * 1024].rearrange("p (a c) -> p a c", a=8),
                    u_bf.ap()[hch].rearrange("a p c -> p a c"), reads=(), writes=(f"wu{pp}",))
                for dc in range(8):
                    lw = WG[:, pp * 1024 + dc * 128: pp * 1024 + (dc + 1) * 128]
                    for tt in range(2):
                        mm(bank(gq + tt), lw,
                           XH[:, dc * stride + off + tt * 512: dc * stride + off + tt * 512 + 512],
                           dc == 0, dc == 7, reads=(f"wg{pp}", f"xh{dc}_{tt}"),
                           writes=(f"pb{gq + tt}",))
                for dc in range(8):
                    lw = WU[:, pp * 1024 + dc * 128: pp * 1024 + (dc + 1) * 128]
                    for tt in range(2):
                        mm(bank(uq + tt), lw,
                           XH[:, dc * stride + off + tt * 512: dc * stride + off + tt * 512 + 512],
                           dc == 0, dc == 7, reads=(f"wu{pp}", f"xh{dc}_{tt}"),
                           writes=(f"pb{uq + tt}",))
                aop(lambda e, b=gq: e.activation(out=KEV[:, 0:1024], in_=bank2(b),
                                                 func=AF.Silu, scale=QD),
                    reads=(f"pb{gq}", f"pb{gq + 1}"), writes=("sg",))
                vop(lambda e, h=hch, b=uq: e.tensor_mul(
                    out=HT[:, h * 1024:(h + 1) * 1024], in0=bank2(b), in1=KEV[:, 0:1024]),
                    reads=(f"pb{uq}", f"pb{uq + 1}", "sg"), writes=("ht",))
            for dc in range(8):
                pp = dc % 2
                dq = 0 if dc % 2 == 0 else 2
                dma("sync", WD2[:, pp * 3072:(pp + 1) * 3072].rearrange("p (a c) -> p a c", a=24),
                    d_bf.ap()[dc].rearrange("a p c -> p a c"), reads=(), writes=(f"wd{pp}",))
                for hch in range(24):
                    lw = WD2[:, pp * 3072 + hch * 128: pp * 3072 + (hch + 1) * 128]
                    for tt in range(2):
                        mm(bank(dq + tt), lw,
                           HT[:, hch * 1024 + tt * 512:(hch * 1024) + tt * 512 + 512],
                           hch == 0, hch == 23, reads=(f"wd{pp}", "ht"),
                           writes=(f"pb{dq + tt}",))
                vop(lambda e, d0=dc, b=dq: e.scalar_tensor_tensor(
                    out=dst[:, d0 * stride + off: d0 * stride + off + 1024],
                    in0=bank2(b), scalar=0.5 * QD * QD,
                    in1=res[:, d0 * stride + off: d0 * stride + off + 1024],
                    op0=ALU.mult, op1=ALU.add),
                    reads=(f"pb{dq}", f"pb{dq + 1}", xtag), writes=(dtag,))

        def load_x(sp):
            T0 = sp * 1024
            dma("sync", XC[:].rearrange("p (a c) -> p a c", a=8),
                xT.ap()[:, T0:T0 + 1024].rearrange("(a p) c -> p a c", p=128),
                reads=(), writes=("p1x",))

        load_x(0)
        for sp in range(8):
            layer_norm("p1x", XC, 2, 1024)
            ffn("p1x", XC, XM, "p1xm", 0, 1024, sp % 2)
            if sp < 7:
                load_x(sp + 1)
            layer_norm("p1xm", XM, 2, 1024)

            def qk_proj(f_lo, n_f, dst_d, dst_base, nrm_tag, nrm_tile):
                for f in range(n_f):
                    fch = f_lo + f
                    wpp = f % 2
                    dma("sync",
                        WQ[:, wpp * 1024:(wpp + 1) * 1024].rearrange("p (a c) -> p a c", a=8),
                        qk_bf.ap()[fch].rearrange("a p c -> p a c"),
                        reads=(), writes=(f"wq{wpp}",))
                    qb = (f % 2) * 2
                    ev = KEV[:, 0:1024] if f % 2 == 0 else VSTG[:, 0:1024]
                    evt = "kev" if f % 2 == 0 else "vstg"
                    for tt in range(2):
                        for dc in range(8):
                            mm(bank(qb + tt),
                               WQ[:, wpp * 1024 + dc * 128: wpp * 1024 + (dc + 1) * 128],
                               XH[:, dc * 1024 + tt * 512: dc * 1024 + tt * 512 + 512],
                               dc == 0, dc == 7, reads=(f"wq{wpp}", f"xh{dc}_{tt}"),
                               writes=(f"pb{qb + tt}",))
                    vop(lambda e, ff=fch, b=qb, e0=ev: e.tensor_scalar_add(
                        out=e0, in0=bank2(b), scalar1=BCOL[:, ff:ff + 1]),
                        reads=(f"pb{qb}", f"pb{qb + 1}", f"bc{fch}"), writes=(evt,))
                    dma("gpsimd",
                        dst_d.ap()[dst_base:dst_base + 2, f].rearrange("s p c -> p s c"),
                        ev.rearrange("p (s c) -> p s c", s=2),
                        reads=(evt,), writes=())
                    aop(lambda e, ff=f, e0=ev: e.activation(
                        out=XQ[:, ff * 1024:(ff + 1) * 1024], in_=e0,
                        func=AF.Square),
                        reads=(evt,), writes=("ksq",))
                for f in range(n_f):
                    for tt in range(2):
                        mm(bank(4 + tt)[0:1, :], ONE128[:],
                           XQ[:, f * 1024 + tt * 512: f * 1024 + tt * 512 + 512],
                           f == 0, f == n_f - 1, reads=("one128", "ksq"),
                           writes=(f"pb{4 + tt}",))
                for tt in range(2):
                    vop(lambda e, t=tt: e.reduce_max(out=TT1[:, 0:1], in_=bank(4 + t)[0:1, :],
                                                     axis=AX.X),
                        reads=(f"pb{4 + tt}",), writes=("tt1",))
                    vop(lambda e, nt=nrm_tile: e.tensor_max(out=nt[:, 0:1], in0=nt[:, 0:1],
                                                            in1=TT1[:, 0:1]),
                        reads=("tt1", nrm_tag), writes=(nrm_tag,))

            qk_proj(8, 8, kK, 2 * sp, "nkm", NKM)

            gop(lambda e: e.memset(
                VSTG[:].rearrange("p (h c) -> p h c", c=65)[:, :, 64:65], 1.0),
                reads=(), writes=("vstg",))
            dma("sync", WV2[:, 0:8192].rearrange("p (g a c) -> p g a c", g=2, a=8),
                v_bf.ap().rearrange("g a p c -> p g a c"),
                reads=(), writes=("wv2",))
            for tk in range(8):
                vbk = (tk % 2) * 2
                for fg in range(2):
                    for dc in range(8):
                        mm(bank(vbk + fg),
                           XH[:, dc * 1024 + tk * 128: dc * 1024 + (tk + 1) * 128],
                           WV2[:, fg * 4096 + dc * 512: fg * 4096 + (dc + 1) * 512],
                           dc == 0, False, reads=("wv2", f"xh{dc}_{tk // 4}"),
                           writes=(f"pb{vbk + fg}",))
                    mm(bank(vbk + fg), ONE1[:], BVR[:, fg * 512:(fg + 1) * 512], False, True,
                       reads=("one1", "bvr"), writes=(f"pb{vbk + fg}",))
                    aop(lambda e, g0=fg, b=vbk: e.activation(
                        out=VSTG[:].rearrange("p (h c) -> p h c", c=65)[
                            :, g0 * 8:(g0 + 1) * 8, 0:64],
                        in_=bank(b + g0).rearrange("p (h c) -> p h c", c=64),
                        func=AF.Copy),
                        reads=(f"pb{vbk + fg}",), writes=("vstg",))
                dma("gpsimd", kV.ap()[2 * sp + tk // 4, tk % 4], VSTG[:],
                    reads=("vstg",), writes=())

            if sp >= 6:
                qk_proj(0, 8, qQ, 2 * (sp - 6), "nqm", NQM)
                dma("gpsimd",
                    xm_d.ap()[:, :, (sp - 6) * 1024:(sp - 6) * 1024 + 1024].rearrange(
                        "a p c -> p a c"),
                    XM[:].rearrange("p (a c) -> p a c", a=8),
                    reads=("p1xm",), writes=())
                if DEV_TAPS and sp == 6:
                    dma("gpsimd", taps["t_xmid"].ap().rearrange("(a p) c -> p a c", p=128),
                        XM[:].rearrange("p (a c) -> p a c", a=8),
                        reads=("p1xm",), writes=())

        if STAGE >= 2:
            vop(lambda e: e.tensor_mul(out=TT1[:, 1:2], in0=NQM[:, 0:1], in1=NKM[:, 0:1]),
                reads=("nqm", "nkm"), writes=("tt1",))
            aop(lambda e: e.activation(out=TT1[:, 2:3], in_=TT1[:, 1:2], func=AF.Ln),
                reads=("tt1",), writes=("tt1",))
            aop(lambda e: e.activation(out=TT1[:, 3:4], in_=TT1[:, 2:3], func=AF.Exp,
                                       scale=0.5),
                reads=("tt1",), writes=("tt1",))
            vop(lambda e: e.tensor_scalar(out=CVA[:, 1:2], in0=TT1[:, 3:4],
                                          scalar1=-SCALE, scalar2=-87.0,
                                          op0=ALU.mult, op1=ALU.add),
                reads=("tt1",), writes=("cva",))
            vop(lambda e: e.tensor_scalar(out=BROW[:], in0=SELR[:], scalar1=87.0,
                                          scalar2=CVA[0:1, 1:2], op0=ALU.mult, op1=ALU.add),
                reads=("selr", "cva"), writes=("brow",))
            vop(lambda e: e.tensor_copy(out=BRB[0:1, 0:64], in_=BROW[:]),
                reads=("brow",), writes=("brb",))
            mm(bank(6)[:, 0:64], ONE1[:], BRB[0:1, 0:64], True, True,
               reads=("one1", "brb"), writes=("pb6",))
            vop(lambda e: e.tensor_copy(out=BIA[:], in_=bank(6)[:, 0:64]),
                reads=("pb6",), writes=("bia",))
            gop(lambda e: e.memset(HT[:, 0:16384], 0.0), reads=(), writes=("kb0", "kb1", "ht"))

            def filler_pieces(s):
                yield lambda: dma(
                    "sync", XC[:, 0:4096].rearrange("p (a c) -> p a c", a=8),
                    xa_d.ap()[:, :, s * 512:(s + 1) * 512].rearrange("a p c -> p a c"),
                    reads=(f"xad{s}",), writes=("p1x",))

                def cast_sq():
                    vop(lambda e: e.tensor_copy(out=XH[:, 0:4096], in_=XC[:, 0:4096]),
                        reads=("p1x",), writes=("f3bf",))
                    aop(lambda e: e.activation(out=HT[:, 16384:20480], in_=XC[:, 0:4096],
                                               func=AF.Square),
                        reads=("p1x",), writes=("f3sq",))
                yield cast_sq

                def stats():
                    for dc in range(8):
                        mm(bank(4)[0:1, :], ONE128[:], XH[:, dc * 512:(dc + 1) * 512],
                           dc == 0, dc == 7, reads=("one128", "f3bf"), writes=("pb4",))
                        mm(bank(5)[0:1, :], ONE128[:],
                           HT[:, 16384 + dc * 512: 16384 + (dc + 1) * 512],
                           dc == 0, dc == 7, reads=("one128", "f3sq"), writes=("pb5",))
                    vop(lambda e: e.tensor_scalar_mul(out=MRow[:], in0=bank(4)[0:1, :],
                                                      scalar1=1.0 / 1024),
                        reads=("pb4",), writes=("mrow",))
                    vop(lambda e: e.tensor_scalar_mul(out=ERow[:], in0=bank(5)[0:1, :],
                                                      scalar1=1.0 / 1024),
                        reads=("pb5",), writes=("erow",))
                    vop(lambda e: e.tensor_mul(out=TRow[:], in0=MRow[:], in1=MRow[:]),
                        reads=("mrow",), writes=("trow",))
                    vop(lambda e: e.tensor_sub(out=ERow[:], in0=ERow[:], in1=TRow[:]),
                        reads=("erow", "trow"), writes=("erow",))
                    aop(lambda e: e.activation(out=TRow[:], in_=ERow[:], func=AF.Ln,
                                               bias=CVA[0:1, 3:4]),
                        reads=("erow", "epsb"), writes=("trow",))
                    aop(lambda e: e.activation(out=RRow[:], in_=TRow[:], func=AF.Exp,
                                               scale=-0.5),
                        reads=("trow",), writes=("rrow",))
                    vop(lambda e: e.tensor_copy(out=ABR[:], in_=RRow[:]),
                        reads=("rrow",), writes=("abr",))
                    vop(lambda e: e.scalar_tensor_tensor(
                        out=BBR[:], in0=MRow[:], scalar=-1.0, in1=RRow[:],
                        op0=ALU.mult, op1=ALU.mult),
                        reads=("mrow", "rrow"), writes=("bbr",))
                    mm(bank(4), ONE1[:], ABR[:], True, True,
                       reads=("one1", "abr"), writes=("pb4",))
                    vop(lambda e: e.tensor_copy(out=ABC[:], in_=bank(4)),
                        reads=("pb4",), writes=("abc",))
                    mm(bank(5), ONE1[:], BBR[:], True, True,
                       reads=("one1", "bbr"), writes=("pb5",))
                    vop(lambda e: e.tensor_copy(out=BBC[:], in_=bank(5)),
                        reads=("pb5",), writes=("bbc",))
                yield stats

                def xhat():
                    for dc in range(8):
                        vop(lambda e, o0=dc * 512: e.tensor_mul(
                            out=TZ[:], in0=XC[:, o0:o0 + 512], in1=ABC[:]),
                            reads=("p1x", "abc"), writes=("tzf",))
                        vop(lambda e, o0=dc * 512: e.tensor_add(
                            out=XH[:, 4096 + o0: 4096 + o0 + 512], in0=TZ[:], in1=BBC[:]),
                            reads=("tzf", "bbc"), writes=("f3h",))
                yield xhat

                for hch in range(24):
                    def gu(hch=hch):
                        pp = hch % 2
                        dma("sync",
                            WG[:, pp * 1024:(pp + 1) * 1024].rearrange("p (a c) -> p a c", a=8),
                            g_bf.ap()[hch].rearrange("a p c -> p a c"),
                            reads=(), writes=(f"wg{pp}",))
                        dma("sync",
                            WU[:, pp * 1024:(pp + 1) * 1024].rearrange("p (a c) -> p a c", a=8),
                            u_bf.ap()[hch].rearrange("a p c -> p a c"),
                            reads=(), writes=(f"wu{pp}",))
                        for dc in range(8):
                            mm(bank(4), WG[:, pp * 1024 + dc * 128: pp * 1024 + (dc + 1) * 128],
                               XH[:, 4096 + dc * 512: 4096 + (dc + 1) * 512],
                               dc == 0, dc == 7, reads=(f"wg{pp}", "f3h"), writes=("pb4",))
                        for dc in range(8):
                            mm(bank(5), WU[:, pp * 1024 + dc * 128: pp * 1024 + (dc + 1) * 128],
                               XH[:, 4096 + dc * 512: 4096 + (dc + 1) * 512],
                               dc == 0, dc == 7, reads=(f"wu{pp}", "f3h"), writes=("pb5",))
                        aop(lambda e: e.activation(out=KEV[:, 0:512], in_=bank(4),
                                                   func=AF.Silu, scale=QD),
                            reads=("pb4",), writes=("sg",))
                        vop(lambda e, h=hch: e.tensor_mul(
                            out=HT[:, 4096 + h * 512: 4096 + (h + 1) * 512],
                            in0=bank(5), in1=KEV[:, 0:512]),
                            reads=("pb5", "sg"), writes=("ht2",))
                    yield gu

                for dc in range(8):
                    def dn(dc=dc, s=s):
                        pp = dc % 2
                        dma("sync",
                            WD2[:, pp * 3072:(pp + 1) * 3072].rearrange("p (a c) -> p a c", a=24),
                            d_bf.ap()[dc].rearrange("a p c -> p a c"),
                            reads=(), writes=(f"wd{pp}",))
                        for hch in range(24):
                            mm(bank(4 + pp),
                               WD2[:, pp * 3072 + hch * 128: pp * 3072 + (hch + 1) * 128],
                               HT[:, 4096 + hch * 512: 4096 + (hch + 1) * 512],
                               hch == 0, hch == 23, reads=(f"wd{pp}", "ht2"),
                               writes=(f"pb{4 + pp}",))
                        vop(lambda e, d0=dc, b=4 + pp: e.scalar_tensor_tensor(
                            out=TZ[:], in0=bank(b), scalar=0.5 * QD * QD,
                            in1=XC[:, d0 * 512:(d0 + 1) * 512],
                            op0=ALU.mult, op1=ALU.add),
                            reads=(f"pb{4 + pp}", "p1x"), writes=("tzf",))
                        dma("gpsimd",
                            outT.ap()[dc * 128:(dc + 1) * 128, s * 512:(s + 1) * 512],
                            TZ[:],
                            reads=("tzf",), writes=("outd",))
                    yield dn

            fillers = []

            def pump(n=1):
                for _ in range(n):
                    if fillers:
                        fillers.pop(0)()

            it_of = {}
            for i, it in enumerate(ITERS):
                it_of.setdefault(it[0], []).append((i, it[1], it[2]))

            for qi in range(4):
                iters = it_of[12 + qi]
                dma("gpsimd",
                    XB[0:64, :].rearrange("p (f hh c) -> p f hh c", f=8, hh=2),
                    qQ.ap()[qi].rearrange("f (hh p) c -> p f hh c", hh=2),
                    reads=(), writes=("qs", "xbf"))
                for pg in range(8):
                    x = 0
                    pend = [None]

                    def flush():
                        if pend[0] is not None:
                            pend[0]()
                            pend[0] = None

                    for ii, (it_idx, ks, kind) in enumerate(iters):
                        pp = ii % 2
                        kb = pp * 1024
                        vb = 2048 + pp * 520
                        dma("gpsimd",
                            HT[0:64, kb:kb + 1024].rearrange("p (hh c) -> p hh c", hh=2),
                            kK.ap()[ks, pg].rearrange("(hh p) c -> p hh c", hh=2),
                            reads=(), writes=(f"kb{pp}",))
                        dma("gpsimd",
                            HT[:, vb:vb + 520].rearrange("p (s c) -> p s c", s=4),
                            kV.ap()[ks][:, :, pg * 130:(pg + 1) * 130],
                            reads=(), writes=(f"vb{pp}",))
                        for hh in range(2):
                            h = 2 * pg + hh
                            for half in range(2):
                                r = x % 2
                                pt = x % 2
                                for s2 in range(2):
                                    sub = half * 2 + s2
                                    mm(PS[r][:, s2 * 512:(s2 + 1) * 512],
                                       HT[:, kb + hh * 512 + sub * 128:
                                          kb + hh * 512 + (sub + 1) * 128],
                                       XB[:, h * 512:(h + 1) * 512],
                                       True, True,
                                       reads=(f"kb{pp}", "qs"),
                                       writes=(f"pb{2 * r + s2}",))
                                aop(lambda e, r0=r, p0=pt, it0=it_idx: e.activation(
                                    out=XQ[:, p0 * 1024:(p0 + 1) * 1024],
                                    in_=PS[r0][:], func=AF.Exp,
                                    bias=BIA[:, it0:it0 + 1], scale=SCALE),
                                    reads=(f"pb{2 * r}", f"pb{2 * r + 1}", "bia"),
                                    writes=(f"pt{pt}",))
                                if kind == 'd':
                                    for s2 in range(2):
                                        sub = half * 2 + s2
                                        vop(lambda e, p0=pt, s0=s2, sb=sub: e.tensor_mul(
                                            out=XQ[:, p0 * 1024 + s0 * 512:
                                                   p0 * 1024 + (s0 + 1) * 512],
                                            in0=XQ[:, p0 * 1024 + s0 * 512:
                                                   p0 * 1024 + (s0 + 1) * 512],
                                            in1=MSK[:, sb * 512:(sb + 1) * 512]),
                                            reads=(f"pt{pt}", "msk"),
                                            writes=(f"pt{pt}",))
                                flush()

                                def mk_av(hh0=hh, half0=half, pt0=pt, pp0=pp, vb0=vb,
                                          first=(ii == 0 and half == 0),
                                          last=(ii == len(iters) - 1 and half == 1)):
                                    def go():
                                        for s2 in range(2):
                                            sub = half0 * 2 + s2
                                            mm(bank(6 + hh0)[0:65, :],
                                               HT[:, vb0 + sub * 130 + hh0 * 65:
                                                  vb0 + sub * 130 + (hh0 + 1) * 65],
                                               XQ[:, pt0 * 1024 + s2 * 512:
                                                  pt0 * 1024 + (s2 + 1) * 512],
                                               first and s2 == 0, last and s2 == 1,
                                               reads=(f"vb{pp0}", f"pt{pt0}"),
                                               writes=(f"pb{6 + hh0}",))
                                    return go
                                pend[0] = mk_av()
                                x += 1
                        pump(1)
                    flush()
                    for hh in range(2):
                        h = 2 * pg + hh
                        vop(lambda e, b=hh: e.reciprocal(out=RRW[:],
                                                         in_=bank(6 + b)[64:65, :]),
                            reads=(f"pb{6 + hh}",), writes=("rrw",))
                        mm(bank(0)[0:64, :], ONE1[:, 0:64], RRW[:], True, True,
                           reads=("one1", "rrw"), writes=("pb0",))
                        vop(lambda e: e.tensor_copy(out=RBS[0:64, 512:1024],
                                                     in_=bank(0)[0:64, :]),
                            reads=("pb0",), writes=("rbs",))
                        vop(lambda e, b=hh, hd=h: e.tensor_mul(
                            out=XQ[(hd % 2) * 64:(hd % 2) * 64 + 64,
                                   4096 + (hd // 2) * 512: 4096 + (hd // 2 + 1) * 512],
                            in0=bank(6 + b)[0:64, :], in1=RBS[0:64, 512:1024]),
                            reads=(f"pb{6 + hh}", "rbs"), writes=("oall",))
                dma("gpsimd", XM[:, 0:4096].rearrange("p (a c) -> p a c", a=8),
                    xm_d.ap()[:, :, qi * 512:(qi + 1) * 512].rearrange("a p c -> p a c"),
                    reads=(), writes=("xms",))
                for dc in range(8):
                    dma("sync", WP[:, 0:1024].rearrange("p (a c) -> p a c", a=8),
                        p_bf.ap()[dc].rearrange("a p c -> p a c"),
                        reads=(), writes=("wp",))
                    for ech in range(8):
                        mm(bank(2), WP[:, ech * 128:(ech + 1) * 128],
                           XQ[:, 4096 + ech * 512: 4096 + (ech + 1) * 512],
                           ech == 0, ech == 7, reads=("wp", "oall"), writes=("pb2",))
                    vop(lambda e, d0=dc: e.tensor_scalar_add(
                        out=TZ[:], in0=bank(2), scalar1=BPC[:, d0:d0 + 1]),
                        reads=("pb2", f"bp{dc}"), writes=("tzf",))
                    vop(lambda e, d0=dc: e.tensor_add(
                        out=XM[:, 4096 + d0 * 512: 4096 + (d0 + 1) * 512],
                        in0=TZ[:], in1=XM[:, d0 * 512:(d0 + 1) * 512]),
                        reads=("tzf", "xms"), writes=("xaa",))
                dma("gpsimd", xa_d.ap()[:, :, qi * 512:(qi + 1) * 512].rearrange("a p c -> p a c"),
                    XM[:, 4096:8192].rearrange("p (a c) -> p a c", a=8),
                    reads=("xaa",), writes=(f"xad{qi}",))
                fillers.extend(filler_pieces(qi))
            while fillers:
                pump(1)
            if DEV_TAPS:
                dma("gpsimd", taps["t_xa"].ap(), xa_d.ap(),
                    reads=("xad0", "xad1", "xad2", "xad3"), writes=())
                dma("gpsimd", taps["t_k"].ap(), kK.ap()[0], reads=(), writes=())
                dma("gpsimd", taps["t_v"].ap(), kV.ap()[0], reads=(), writes=())
                dma("gpsimd", taps["t_q"].ap(), qQ.ap()[0], reads=(), writes=())
                dma("gpsimd", taps["t_c"].ap(), BIA[:], reads=("bia",), writes=())

        P.op("sync", lambda e: e.nop(),
             reads=tuple(k for k, v in P.last_w.items() if v is not None), writes=())

        _emit(nc, P, ctx)

    nc.compile()
    return nc


def _emit(nc, P, ctx):
    sems = {e: ctx.enter_context(nc.semaphore(f"sem_{e}")) for e in Prog.ENGS}
    lp = nc.allow_low_precision(
        reason="bf16 matmul operands with fp32 PSUM + fp32 elementwise interiors; "
               "bf16 stores are the intended operand precision")
    with lp, nc.Block() as block:
        def section(eng_name):
            def body(eng):
                for waits, emit, inc in P.plan[eng_name]:
                    for (src, val) in waits:
                        eng.wait_ge(sems[src], val)
                    ins = emit(eng)
                    if ins is None:
                        ins = eng.nop()
                    ins.then_inc(sems[eng_name], inc)
            return body
        block.sync(section("sync"))
        block.tensor(section("tensor"))
        block.vector(section("vector"))
        block.scalar(section("scalar"))
        block.gpsimd(section("gpsimd"))


_NC_CACHE = None


def kernel(**inputs):
    global _NC_CACHE
    cores = _host_inputs(**inputs)
    if _NC_CACHE is None:
        _NC_CACHE = build_nc()
    res = run_bass_kernel_spmd(_NC_CACHE, cores, core_ids=list(range(8)))
    return _host_assemble(res.results)


if __name__ == "__main__":
    build_nc()
    print("built ok")


# version 13
# speedup vs baseline: 1.0219x; 1.0219x over previous
import os
from contextlib import ExitStack

import numpy as np

import concourse.bacc as bacc
import concourse.mybir as mybir
from concourse.bass_utils import run_bass_kernel_spmd

F32 = mybir.dt.float32
BF16 = mybir.dt.bfloat16
I32 = mybir.dt.int32
AF = mybir.ActivationFunctionType
ALU = mybir.AluOpType
AX = mybir.AxisListType

D = 1024
NS = 16
SL = 512
SEQ = 8192
SCALE = 0.125
QD = 1.0 / 127.0

ITERS = []
for _qi, _caps in ((12, 3), (13, 6), (14, 9), (15, 12)):
    for _ks in range(_caps):
        ITERS.append((_qi, _ks, 'g'))
    for _os in range(12, _qi):
        ITERS.append((_qi, _os, 'f'))
    ITERS.append((_qi, _qi, 'd'))

STAGE = int(os.environ.get("MK_STAGE", "3"))
DEV_TAPS = os.environ.get("MK_TAPS", "") == "1"



def _core_slots(g):
    owned = [g, g + 4, g + 8, g + 12]
    nonowned = [c for c in range(16) if c not in owned]
    return nonowned + owned, owned


def _selectors(perm):
    sel = np.zeros((1, 64), np.float32)
    for i, (qi, ks, kind) in enumerate(ITERS):
        sel[0, i] = 1.0 if (kind != 'g' or perm[ks] < perm[qi]) else 0.0
    return sel


def _host_inputs(L0_attn_hidden_states, L0_attn_c_attn_kernel, L0_attn_c_attn_bias,
                 L0_attn_c_proj_kernel, L0_attn_c_proj_bias,
                 L0_ffn_w_gate, L0_ffn_w_up, L0_ffn_w_down):
    x = np.asarray(L0_attn_hidden_states)
    wqkv = np.asarray(L0_attn_c_attn_kernel)
    battn = np.ascontiguousarray(np.asarray(L0_attn_c_attn_bias, np.float32))
    wproj = np.asarray(L0_attn_c_proj_kernel)
    bproj = np.ascontiguousarray(np.asarray(L0_attn_c_proj_bias, np.float32))

    def tile_lhsT(w_T, n_out, n_in):
        t = w_T.reshape(n_in, 128, n_out, 128)
        return np.ascontiguousarray(t.transpose(2, 0, 1, 3))

    wg_t = tile_lhsT(np.asarray(L0_ffn_w_gate).T, 24, 8)
    wu_t = tile_lhsT(np.asarray(L0_ffn_w_up).T, 24, 8)
    wd_t = tile_lhsT(np.asarray(L0_ffn_w_down).T, 8, 24)
    wqkvT = wqkv.T
    wqk_t = tile_lhsT(np.ascontiguousarray(wqkvT[:, :2048]), 16, 8)
    wv = np.ascontiguousarray(wqkvT[:, 2048:])
    wv_t = np.ascontiguousarray(wv.reshape(8, 128, 2, 512).transpose(2, 0, 1, 3))
    wp_t = tile_lhsT(np.ascontiguousarray(wproj.T), 8, 8)

    cores = []
    for b in range(2):
        for g in range(4):
            perm, owned = _core_slots(g)
            idx = np.concatenate([np.arange(c * SL, (c + 1) * SL) for c in perm])
            xp = np.ascontiguousarray(x[b][idx].T)
            cores.append({
                "xT": xp, "sel": _selectors(perm),
                "wg_t": wg_t, "wu_t": wu_t, "wd_t": wd_t,
                "wqk_t": wqk_t, "wv_t": wv_t, "wp_t": wp_t,
                "battn": battn, "bproj": bproj,
            })
    return cores


def _host_assemble(outs):
    final = np.empty((2, SEQ, D), np.float32)
    for b in range(2):
        for g in range(4):
            perm, owned = _core_slots(g)
            o = np.asarray(outs[b * 4 + g]["outT"], np.float32)
            for i, c in enumerate(owned):
                final[b, c * SL:(c + 1) * SL, :] = o[:, i * SL:(i + 1) * SL].T
    return final



CONF = {
    "xbf": {"qs"}, "qs": {"xbf"},
    "xsq": {"ksq", "pt0", "pt1", "pt2", "oall"},
    "ksq": {"xsq", "pt0", "pt1", "pt2"},
    "pt0": {"xsq", "ksq"}, "pt1": {"xsq", "ksq"}, "pt2": {"xsq", "ksq"},
    "oall": {"xsq"},
    "p1x": {"cvf", "bvf"}, "cvf": {"p1x", "bvf"}, "bvf": {"p1x", "cvf"},
    "wp": {"wq0", "wq1"}, "wq0": {"wp"}, "wq1": {"wp"},
    "erow": {"rrow"}, "rrow": {"erow"},
    "rrw": {"abr"}, "abr": {"rrw"},
    "f3bf": {"xh0_0", "xh0_1", "xh1_0", "xh1_1", "xh2_0", "xh2_1", "xh3_0", "xh3_1"},
    "xh0_0": {"f3bf"}, "xh0_1": {"f3bf"}, "xh1_0": {"f3bf"}, "xh1_1": {"f3bf"},
    "xh2_0": {"f3bf"}, "xh2_1": {"f3bf"}, "xh3_0": {"f3bf"}, "xh3_1": {"f3bf"},
    "f3h": {"xh4_0", "xh4_1", "xh5_0", "xh5_1", "xh6_0", "xh6_1", "xh7_0", "xh7_1"},
    "xh4_0": {"f3h"}, "xh4_1": {"f3h"}, "xh5_0": {"f3h"}, "xh5_1": {"f3h"},
    "xh6_0": {"f3h"}, "xh6_1": {"f3h"}, "xh7_0": {"f3h"}, "xh7_1": {"f3h"},
    "ht2": {"ht", "wv2"}, "f3sq": {"ht"},
    "rbs": {"kev", "sg"},
    "brb": {"bbr"}, "bbr": {"brb"},
    "p1xm": {"xms", "xaa", "cvi"}, "xms": {"p1xm"}, "xaa": {"p1xm"},
    "cvi": {"p1xm"},
    "ht": {"kb0", "kb1", "vb0", "vb1", "wv2", "ht2", "f3sq"},
    "kb0": {"ht", "wv2"}, "kb1": {"ht", "wv2"},
    "vb0": {"ht", "wv2"}, "vb1": {"ht", "wv2"},
    "wv2": {"ht", "kb0", "kb1", "vb0", "vb1", "ht2"},
    "sg": {"kev"}, "kev": {"sg"},
    "cvo0": {"xsq", "ksq", "pt0", "pt1", "pt2", "oall", "cvo1"},
    "cvo1": {"xsq", "ksq", "pt0", "pt1", "pt2", "oall", "cvo0"},
}


class Prog:
    ENGS = ("sync", "tensor", "vector", "scalar", "gpsimd")

    def __init__(self):
        self.plan = {e: [] for e in self.ENGS}
        self.cnt = {e: 0 for e in self.ENGS}
        self.last_w = {}
        self.readers = {}
        self.waited = {e: {} for e in self.ENGS}

    def _names(self, t):
        return (t,) + tuple(CONF.get(t, ()))

    def op(self, eng, emit, reads=(), writes=(), inc=1, pe_chain=False):
        needs = {}

        def need(w):
            if w:
                needs[w[0]] = max(needs.get(w[0], 0), w[1])

        for t in reads:
            need(self.last_w.get(t))
            for o in CONF.get(t, ()):
                need(self.last_w.get(o))
        for t in writes:
            for n in self._names(t):
                need(self.last_w.get(n))
                for r in self.readers.get(n, ()):
                    need(r)
        waits = []
        for src, val in needs.items():
            if src == eng and (pe_chain or eng == "tensor"):
                continue
            if self.waited[eng].get(src, 0) >= val:
                continue
            waits.append((src, val))
            self.waited[eng][src] = val
        self.cnt[eng] += inc
        my = (eng, self.cnt[eng])
        for t in writes:
            self.last_w[t] = my
            self.readers[t] = []
        for t in reads:
            self.readers.setdefault(t, []).append(my)
        self.plan[eng].append((waits, emit, inc))



def build_nc():
    nc = bacc.Bacc("TRN2", target_bir_lowering=False)

    xT = nc.dram_tensor("xT", [D, SEQ], F32, kind="ExternalInput")
    selT = nc.dram_tensor("sel", [1, 64], F32, kind="ExternalInput")
    wg_t = nc.dram_tensor("wg_t", [24, 8, 128, 128], I32, kind="ExternalInput")
    wu_t = nc.dram_tensor("wu_t", [24, 8, 128, 128], I32, kind="ExternalInput")
    wd_t = nc.dram_tensor("wd_t", [8, 24, 128, 128], I32, kind="ExternalInput")
    wqk_t = nc.dram_tensor("wqk_t", [16, 8, 128, 128], F32, kind="ExternalInput")
    wv_t = nc.dram_tensor("wv_t", [2, 8, 128, 512], F32, kind="ExternalInput")
    wp_t = nc.dram_tensor("wp_t", [8, 8, 128, 128], F32, kind="ExternalInput")
    battn = nc.dram_tensor("battn", [3072], F32, kind="ExternalInput")
    bprojT = nc.dram_tensor("bproj", [D], F32, kind="ExternalInput")
    outT = nc.dram_tensor("outT", [D, 4 * SL], F32, kind="ExternalOutput")
    taps = {}
    if DEV_TAPS:
        taps["t_xmid"] = nc.dram_tensor("t_xmid", [D, 1024], F32, kind="ExternalOutput")
        taps["t_xa"] = nc.dram_tensor("t_xa", [8, 128, 4 * SL], F32, kind="ExternalOutput")
        taps["t_k"] = nc.dram_tensor("t_k", [8, 128, SL], BF16, kind="ExternalOutput")
        taps["t_v"] = nc.dram_tensor("t_v", [4, 128, 1040], BF16, kind="ExternalOutput")
        taps["t_q"] = nc.dram_tensor("t_q", [8, 128, SL], BF16, kind="ExternalOutput")
        taps["t_c"] = nc.dram_tensor("t_c", [128, 64], F32, kind="ExternalOutput")
        taps["t_o"] = nc.dram_tensor("t_o", [128, 512], F32, kind="ExternalOutput")

    g_bf = nc.dram_tensor("g_bf", [24, 8, 128, 128], BF16, kind="Internal")
    u_bf = nc.dram_tensor("u_bf", [24, 8, 128, 128], BF16, kind="Internal")
    d_bf = nc.dram_tensor("d_bf", [8, 24, 128, 128], BF16, kind="Internal")
    qk_bf = nc.dram_tensor("qk_bf", [16, 8, 128, 128], BF16, kind="Internal")
    v_bf = nc.dram_tensor("v_bf", [2, 8, 128, 512], BF16, kind="Internal")
    p_bf = nc.dram_tensor("p_bf", [8, 8, 128, 128], BF16, kind="Internal")
    kK = nc.dram_tensor("kK", [NS, 8, 128, SL], BF16, kind="Internal")
    kV = nc.dram_tensor("kV", [NS, 4, 128, 1040], BF16, kind="Internal")
    qQ = nc.dram_tensor("qQ", [4, 8, 128, SL], BF16, kind="Internal")
    xm_d = nc.dram_tensor("xm_d", [8, 128, 4 * SL], F32, kind="Internal")
    xa_d = nc.dram_tensor("xa_d", [8, 128, 4 * SL], F32, kind="Internal")

    P = Prog()

    with ExitStack() as ctx:
        sbt = lambda n, s, d: ctx.enter_context(nc.sbuf_tensor(n, s, d))
        pst = lambda n: ctx.enter_context(nc.psum_tensor(n, [128, 1024], F32))

        XC = sbt("XC", [128, 8192], F32)
        XM = sbt("XM", [128, 8192], F32)
        HT = sbt("HT", [128, 24832], BF16)
        XB = sbt("XB", [128, 8192], BF16)
        XH = sbt("XH", [128, 8192], BF16)
        XQ = sbt("XQ", [128, 8192], BF16)
        CVO = XQ
        WV2 = HT
        WG = sbt("WG", [128, 2048], BF16)
        WU = sbt("WU", [128, 2048], BF16)
        WD2 = sbt("WD2", [128, 6144], BF16)
        WQ = sbt("WQ", [128, 2048], BF16)
        WP = WQ
        KEV = sbt("KEV", [128, 1024], BF16)
        RBS = KEV
        VSTG = sbt("VSTG", [128, 1040], BF16)
        TZ = sbt("TZ", [128, 512], F32)
        ABC = sbt("ABC", [128, 512], BF16)
        BBC = sbt("BBC", [128, 512], BF16)
        MSK = sbt("MSK", [128, 2048], BF16)
        BIA = sbt("BIA", [128, 64], F32)
        BCOL = sbt("BCOL", [128, 32], F32)
        BPC = sbt("BPC", [128, 8], F32)
        ONE1 = sbt("ONE1", [1, 128], BF16)
        ONE128 = sbt("ONE128", [128, 1], BF16)
        BVR = sbt("BVR", [1, 1024], BF16)
        SELR = sbt("SELR", [1, 64], F32)
        MRow = sbt("MRow", [1, 512], F32)
        ERow = sbt("ERow", [1, 512], F32)
        TRow = sbt("TRow", [1, 512], F32)
        RRow = ERow
        ABR = sbt("ABR", [1, 512], BF16)
        BBR = sbt("BBR", [1, 512], BF16)
        NKM = sbt("NKM", [1, 4], F32)
        NQM = sbt("NQM", [1, 4], F32)
        TT1 = sbt("TT1", [1, 4], F32)
        CVA = sbt("CVA", [1, 4], F32)
        BROW = sbt("BROW", [1, 64], F32)
        BRB = BBR
        RRW = ABR

        PS = [pst(f"PS{i}") for i in range(4)]

        def bank(i):
            return PS[i // 2][:, (i % 2) * 512:(i % 2) * 512 + 512]

        def bank2(i):
            return PS[i // 2][:]

        def dma(qeng, out, in_, reads, writes):
            P.op(qeng, lambda e, o=out, i=in_: e.dma_start(out=o, in_=i),
                 reads=reads, writes=writes, inc=16)

        def mm(out, lhsT, rhs, start, stop, reads, writes):
            P.op("tensor",
                 lambda e, o=out, l=lhsT, r=rhs, s1=start, s2=stop:
                     e.matmul(out=o, lhsT=l, rhs=r, start=s1, stop=s2),
                 reads=reads, writes=writes, inc=1, pe_chain=True)

        def vop(emit, reads, writes):
            P.op("vector", emit, reads=reads, writes=writes, inc=1)

        def aop(emit, reads, writes):
            P.op("scalar", emit, reads=reads, writes=writes, inc=1)

        def gop(emit, reads, writes, inc=1):
            P.op("gpsimd", emit, reads=reads, writes=writes, inc=inc)

        def convert(src, dst, n_outer, n_a, n_c, is_int):
            n = n_a * n_c
            for i in range(n_outer):
                pp = i % 2
                if is_int:
                    stage = XM[:, pp * 4096: pp * 4096 + n].bitcast(I32)
                    stag = "cvi"
                else:
                    stage = XC[:, pp * 4096: pp * 4096 + n]
                    stag = "cvf"
                dma("sync", stage.rearrange("p (a c) -> p a c", a=n_a),
                    src.ap()[i].rearrange("a p c -> p a c"),
                    reads=(), writes=(stag + str(pp),))
                vop(lambda e, s=stage, n0=n, q=pp: e.tensor_copy(
                    out=CVO[:, q * 2048: q * 2048 + n0] if n0 <= 2048 else CVO[:, 0:n0],
                    in_=s),
                    reads=(stag + str(pp),), writes=(f"cvo{pp}" if n <= 2048 else "cvo0",))
                co = CVO[:, pp * 2048: pp * 2048 + n] if n <= 2048 else CVO[:, 0:n]
                dma("gpsimd", dst.ap()[i].rearrange("a p c -> p a c"),
                    co.rearrange("p (a c) -> p a c", a=n_a),
                    reads=((f"cvo{pp}" if n <= 2048 else "cvo0"),), writes=())

        convert(wg_t, g_bf, 24, 8, 128, True)
        convert(wu_t, u_bf, 24, 8, 128, True)
        convert(wd_t, d_bf, 8, 24, 128, True)
        convert(wqk_t, qk_bf, 16, 8, 128, False)
        convert(wv_t, v_bf, 2, 8, 512, False)
        convert(wp_t, p_bf, 8, 8, 128, False)

        for f in range(24):
            dma("sync", BCOL[:, f:f + 1],
                battn.ap()[f * 128:(f + 1) * 128].rearrange("(p o) -> p o", o=1),
                reads=(), writes=(f"bc{f}",))
        for dc in range(8):
            dma("sync", BPC[:, dc:dc + 1],
                bprojT.ap()[dc * 128:(dc + 1) * 128].rearrange("(p o) -> p o", o=1),
                reads=(), writes=(f"bp{dc}",))
        dma("sync", XC[0:1, 0:1024], battn.ap()[2048:3072].rearrange("(o c) -> o c", o=1),
            reads=(), writes=("bvf",))
        vop(lambda e: e.tensor_copy(out=BVR[:], in_=XC[0:1, 0:1024]),
            reads=("bvf",), writes=("bvr",))
        dma("sync", SELR[:], selT.ap(), reads=(), writes=("selr",))

        gop(lambda e: e.memset(ONE1[:], 1.0), reads=(), writes=("one1",))
        gop(lambda e: e.memset(CVA[0:1, 3:4], 1e-5), reads=(), writes=("epsb",))
        gop(lambda e: e.memset(ONE128[:], 1.0), reads=(), writes=("one128",))
        gop(lambda e: e.memset(NKM[:], 0.0), reads=(), writes=("nkm",))
        gop(lambda e: e.memset(NQM[:], 0.0), reads=(), writes=("nqm",))
        gop(lambda e: e.memset(VSTG[:], 0.0), reads=(), writes=("vstg",))
        gop(lambda e: e.memset(
            VSTG[:].rearrange("p (h c) -> p h c", c=65)[:, :, 64:65], 1.0),
            reads=(), writes=("vstg",))
        gop(lambda e: e.iota(TZ[:], [[1, 512]], base=0, channel_multiplier=-1,
                             allow_small_or_imprecise_dtypes=True),
            reads=(), writes=("tzf",))
        for r in range(4):
            vop(lambda e, rr=r: e.tensor_single_scalar(
                out=MSK[:, rr * 512:(rr + 1) * 512], in_=TZ[:],
                scalar=float(rr * 128), op=ALU.is_ge),
                reads=("tzf",), writes=("msk",))

        def layer_norm(xtag, src, n_tt, stride):
            vop(lambda e, n=n_tt: e.tensor_copy(
                out=XB[:, 0:8 * stride], in_=src[:, 0:8 * stride]),
                reads=(xtag,), writes=("xbf",))
            aop(lambda e, n=n_tt: e.activation(
                out=XQ[:, 0:8 * stride], in_=src[:, 0:8 * stride], func=AF.Square),
                reads=(xtag,), writes=("xsq",))
            for tt in range(n_tt):
                off = tt * 512
                for dc in range(8):
                    mm(bank(6)[0:1, :], ONE128[:], XB[:, dc * stride + off: dc * stride + off + 512],
                       dc == 0, dc == 7, reads=("one128", "xbf"), writes=("pb6",))
                    mm(bank(7)[0:1, :], ONE128[:], XQ[:, dc * stride + off: dc * stride + off + 512],
                       dc == 0, dc == 7, reads=("one128", "xsq"), writes=("pb7",))
                vop(lambda e: e.tensor_scalar_mul(out=MRow[:], in0=bank(6)[0:1, :],
                                                  scalar1=1.0 / 1024),
                    reads=("pb6",), writes=("mrow",))
                vop(lambda e: e.tensor_scalar_mul(out=ERow[:], in0=bank(7)[0:1, :],
                                                  scalar1=1.0 / 1024),
                    reads=("pb7",), writes=("erow",))
                vop(lambda e: e.tensor_mul(out=TRow[:], in0=MRow[:], in1=MRow[:]),
                    reads=("mrow",), writes=("trow",))
                vop(lambda e: e.tensor_sub(out=ERow[:], in0=ERow[:], in1=TRow[:]),
                    reads=("erow", "trow"), writes=("erow",))
                aop(lambda e: e.activation(out=TRow[:], in_=ERow[:], func=AF.Ln,
                                           bias=CVA[0:1, 3:4]),
                    reads=("erow", "epsb"), writes=("trow",))
                aop(lambda e: e.activation(out=RRow[:], in_=TRow[:], func=AF.Exp, scale=-0.5),
                    reads=("trow",), writes=("rrow",))
                vop(lambda e: e.tensor_copy(out=ABR[:], in_=RRow[:]),
                    reads=("rrow",), writes=("abr",))
                vop(lambda e: e.scalar_tensor_tensor(
                    out=BBR[:], in0=MRow[:], scalar=-1.0, in1=RRow[:],
                    op0=ALU.mult, op1=ALU.mult),
                    reads=("mrow", "rrow"), writes=("bbr",))
                mm(bank(6), ONE1[:], ABR[:], True, True, reads=("one1", "abr"), writes=("pb6",))
                vop(lambda e: e.tensor_copy(out=ABC[:], in_=bank(6)), reads=("pb6",), writes=("abc",))
                mm(bank(7), ONE1[:], BBR[:], True, True, reads=("one1", "bbr"), writes=("pb7",))
                vop(lambda e: e.tensor_copy(out=BBC[:], in_=bank(7)), reads=("pb7",), writes=("bbc",))
                for dc in range(8):
                    o = dc * stride + off
                    vop(lambda e, o0=o: e.tensor_mul(
                        out=TZ[:], in0=src[:, o0:o0 + 512], in1=ABC[:]),
                        reads=(xtag, "abc"), writes=("tzf",))
                    vop(lambda e, o0=o: e.tensor_add(
                        out=XH[:, o0:o0 + 512], in0=TZ[:], in1=BBC[:]),
                        reads=("tzf", "bbc"), writes=(f"xh{dc}_{tt}",))

        def ffn(xtag, res, dst, dtag, off, stride, wpar):
            for hch in range(24):
                pp = (hch + wpar) % 2
                gq = 0 if hch % 2 == 0 else 4
                uq = 2 if hch % 2 == 0 else 6
                dma("sync", WG[:, pp * 1024:(pp + 1) * 1024].rearrange("p (a c) -> p a c", a=8),
                    g_bf.ap()[hch].rearrange("a p c -> p a c"), reads=(), writes=(f"wg{pp}",))
                dma("sync", WU[:, pp * 1024:(pp + 1) * 1024].rearrange("p (a c) -> p a c", a=8),
                    u_bf.ap()[hch].rearrange("a p c -> p a c"), reads=(), writes=(f"wu{pp}",))
                for dc in range(8):
                    lw = WG[:, pp * 1024 + dc * 128: pp * 1024 + (dc + 1) * 128]
                    for tt in range(2):
                        mm(bank(gq + tt), lw,
                           XH[:, dc * stride + off + tt * 512: dc * stride + off + tt * 512 + 512],
                           dc == 0, dc == 7, reads=(f"wg{pp}", f"xh{dc}_{tt}"),
                           writes=(f"pb{gq + tt}",))
                for dc in range(8):
                    lw = WU[:, pp * 1024 + dc * 128: pp * 1024 + (dc + 1) * 128]
                    for tt in range(2):
                        mm(bank(uq + tt), lw,
                           XH[:, dc * stride + off + tt * 512: dc * stride + off + tt * 512 + 512],
                           dc == 0, dc == 7, reads=(f"wu{pp}", f"xh{dc}_{tt}"),
                           writes=(f"pb{uq + tt}",))
                aop(lambda e, b=gq: e.activation(out=KEV[:, 0:1024], in_=bank2(b),
                                                 func=AF.Silu, scale=QD),
                    reads=(f"pb{gq}", f"pb{gq + 1}"), writes=("sg",))
                vop(lambda e, h=hch, b=uq: e.tensor_mul(
                    out=HT[:, h * 1024:(h + 1) * 1024], in0=bank2(b), in1=KEV[:, 0:1024]),
                    reads=(f"pb{uq}", f"pb{uq + 1}", "sg"), writes=("ht",))
            for dc in range(8):
                pp = dc % 2
                dq = 0 if dc % 2 == 0 else 2
                dma("sync", WD2[:, pp * 3072:(pp + 1) * 3072].rearrange("p (a c) -> p a c", a=24),
                    d_bf.ap()[dc].rearrange("a p c -> p a c"), reads=(), writes=(f"wd{pp}",))
                for hch in range(24):
                    lw = WD2[:, pp * 3072 + hch * 128: pp * 3072 + (hch + 1) * 128]
                    for tt in range(2):
                        mm(bank(dq + tt), lw,
                           HT[:, hch * 1024 + tt * 512:(hch * 1024) + tt * 512 + 512],
                           hch == 0, hch == 23, reads=(f"wd{pp}", "ht"),
                           writes=(f"pb{dq + tt}",))
                vop(lambda e, d0=dc, b=dq: e.scalar_tensor_tensor(
                    out=dst[:, d0 * stride + off: d0 * stride + off + 1024],
                    in0=bank2(b), scalar=0.5 * QD * QD,
                    in1=res[:, d0 * stride + off: d0 * stride + off + 1024],
                    op0=ALU.mult, op1=ALU.add),
                    reads=(f"pb{dq}", f"pb{dq + 1}", xtag), writes=(dtag,))

        def load_x(sp):
            T0 = sp * 1024
            dma("sync", XC[:].rearrange("p (a c) -> p a c", a=8),
                xT.ap()[:, T0:T0 + 1024].rearrange("(a p) c -> p a c", p=128),
                reads=(), writes=("p1x",))

        load_x(0)
        for sp in range(8):
            layer_norm("p1x", XC, 2, 1024)
            ffn("p1x", XC, XM, "p1xm", 0, 1024, sp % 2)
            if sp < 7:
                load_x(sp + 1)
            layer_norm("p1xm", XM, 2, 1024)

            def qk_proj(f_lo, n_f, dst_d, dst_base, nrm_tag, nrm_tile):
                for f in range(n_f):
                    fch = f_lo + f
                    wpp = f % 2
                    dma("sync",
                        WQ[:, wpp * 1024:(wpp + 1) * 1024].rearrange("p (a c) -> p a c", a=8),
                        qk_bf.ap()[fch].rearrange("a p c -> p a c"),
                        reads=(), writes=(f"wq{wpp}",))
                    qb = (f % 2) * 2
                    ev = KEV[:, 0:1024] if f % 2 == 0 else VSTG[:, 0:1024]
                    evt = "kev" if f % 2 == 0 else "vstg"
                    for tt in range(2):
                        for dc in range(8):
                            mm(bank(qb + tt),
                               WQ[:, wpp * 1024 + dc * 128: wpp * 1024 + (dc + 1) * 128],
                               XH[:, dc * 1024 + tt * 512: dc * 1024 + tt * 512 + 512],
                               dc == 0, dc == 7, reads=(f"wq{wpp}", f"xh{dc}_{tt}"),
                               writes=(f"pb{qb + tt}",))
                    vop(lambda e, ff=fch, b=qb, e0=ev: e.tensor_scalar_add(
                        out=e0, in0=bank2(b), scalar1=BCOL[:, ff:ff + 1]),
                        reads=(f"pb{qb}", f"pb{qb + 1}", f"bc{fch}"), writes=(evt,))
                    dma("gpsimd",
                        dst_d.ap()[dst_base:dst_base + 2, f].rearrange("s p c -> p s c"),
                        ev.rearrange("p (s c) -> p s c", s=2),
                        reads=(evt,), writes=())
                    aop(lambda e, ff=f, e0=ev: e.activation(
                        out=XQ[:, ff * 1024:(ff + 1) * 1024], in_=e0,
                        func=AF.Square),
                        reads=(evt,), writes=("ksq",))
                for f in range(n_f):
                    for tt in range(2):
                        mm(bank(4 + tt)[0:1, :], ONE128[:],
                           XQ[:, f * 1024 + tt * 512: f * 1024 + tt * 512 + 512],
                           f == 0, f == n_f - 1, reads=("one128", "ksq"),
                           writes=(f"pb{4 + tt}",))
                for tt in range(2):
                    vop(lambda e, t=tt: e.reduce_max(out=TT1[:, 0:1], in_=bank(4 + t)[0:1, :],
                                                     axis=AX.X),
                        reads=(f"pb{4 + tt}",), writes=("tt1",))
                    vop(lambda e, nt=nrm_tile: e.tensor_max(out=nt[:, 0:1], in0=nt[:, 0:1],
                                                            in1=TT1[:, 0:1]),
                        reads=("tt1", nrm_tag), writes=(nrm_tag,))

            qk_proj(8, 8, kK, 2 * sp, "nkm", NKM)

            gop(lambda e: e.memset(
                VSTG[:].rearrange("p (h c) -> p h c", c=65)[:, :, 64:65], 1.0),
                reads=(), writes=("vstg",))
            dma("sync", WV2[:, 0:8192].rearrange("p (g a c) -> p g a c", g=2, a=8),
                v_bf.ap().rearrange("g a p c -> p g a c"),
                reads=(), writes=("wv2",))
            for tk in range(8):
                vbk = (tk % 2) * 2
                for fg in range(2):
                    for dc in range(8):
                        mm(bank(vbk + fg),
                           XH[:, dc * 1024 + tk * 128: dc * 1024 + (tk + 1) * 128],
                           WV2[:, fg * 4096 + dc * 512: fg * 4096 + (dc + 1) * 512],
                           dc == 0, False, reads=("wv2", f"xh{dc}_{tk // 4}"),
                           writes=(f"pb{vbk + fg}",))
                    mm(bank(vbk + fg), ONE1[:], BVR[:, fg * 512:(fg + 1) * 512], False, True,
                       reads=("one1", "bvr"), writes=(f"pb{vbk + fg}",))
                    aop(lambda e, g0=fg, b=vbk: e.activation(
                        out=VSTG[:].rearrange("p (h c) -> p h c", c=65)[
                            :, g0 * 8:(g0 + 1) * 8, 0:64],
                        in_=bank(b + g0).rearrange("p (h c) -> p h c", c=64),
                        func=AF.Copy),
                        reads=(f"pb{vbk + fg}",), writes=("vstg",))
                dma("gpsimd", kV.ap()[2 * sp + tk // 4, tk % 4], VSTG[:],
                    reads=("vstg",), writes=())

            if sp >= 6:
                qk_proj(0, 8, qQ, 2 * (sp - 6), "nqm", NQM)
                dma("gpsimd",
                    xm_d.ap()[:, :, (sp - 6) * 1024:(sp - 6) * 1024 + 1024].rearrange(
                        "a p c -> p a c"),
                    XM[:].rearrange("p (a c) -> p a c", a=8),
                    reads=("p1xm",), writes=())
                if DEV_TAPS and sp == 6:
                    dma("gpsimd", taps["t_xmid"].ap().rearrange("(a p) c -> p a c", p=128),
                        XM[:].rearrange("p (a c) -> p a c", a=8),
                        reads=("p1xm",), writes=())

        if STAGE >= 2:
            vop(lambda e: e.tensor_mul(out=TT1[:, 1:2], in0=NQM[:, 0:1], in1=NKM[:, 0:1]),
                reads=("nqm", "nkm"), writes=("tt1",))
            aop(lambda e: e.activation(out=TT1[:, 2:3], in_=TT1[:, 1:2], func=AF.Ln),
                reads=("tt1",), writes=("tt1",))
            aop(lambda e: e.activation(out=TT1[:, 3:4], in_=TT1[:, 2:3], func=AF.Exp,
                                       scale=0.5),
                reads=("tt1",), writes=("tt1",))
            vop(lambda e: e.tensor_scalar(out=CVA[:, 1:2], in0=TT1[:, 3:4],
                                          scalar1=-SCALE, scalar2=-87.0,
                                          op0=ALU.mult, op1=ALU.add),
                reads=("tt1",), writes=("cva",))
            vop(lambda e: e.tensor_scalar(out=BROW[:], in0=SELR[:], scalar1=87.0,
                                          scalar2=CVA[0:1, 1:2], op0=ALU.mult, op1=ALU.add),
                reads=("selr", "cva"), writes=("brow",))
            vop(lambda e: e.tensor_copy(out=BRB[0:1, 0:64], in_=BROW[:]),
                reads=("brow",), writes=("brb",))
            mm(bank(6)[:, 0:64], ONE1[:], BRB[0:1, 0:64], True, True,
               reads=("one1", "brb"), writes=("pb6",))
            vop(lambda e: e.tensor_copy(out=BIA[:], in_=bank(6)[:, 0:64]),
                reads=("pb6",), writes=("bia",))
            gop(lambda e: e.memset(HT[:, 0:16384], 0.0), reads=(), writes=("kb0", "kb1", "ht"))

            def filler_pieces(s):
                yield lambda: dma(
                    "sync", XC[:, 0:4096].rearrange("p (a c) -> p a c", a=8),
                    xa_d.ap()[:, :, s * 512:(s + 1) * 512].rearrange("a p c -> p a c"),
                    reads=(f"xad{s}",), writes=("p1x",))

                def cast_sq():
                    vop(lambda e: e.tensor_copy(out=XH[:, 0:4096], in_=XC[:, 0:4096]),
                        reads=("p1x",), writes=("f3bf",))
                    aop(lambda e: e.activation(out=HT[:, 16384:20480], in_=XC[:, 0:4096],
                                               func=AF.Square),
                        reads=("p1x",), writes=("f3sq",))
                yield cast_sq

                def stats():
                    for dc in range(8):
                        mm(bank(4)[0:1, :], ONE128[:], XH[:, dc * 512:(dc + 1) * 512],
                           dc == 0, dc == 7, reads=("one128", "f3bf"), writes=("pb4",))
                        mm(bank(5)[0:1, :], ONE128[:],
                           HT[:, 16384 + dc * 512: 16384 + (dc + 1) * 512],
                           dc == 0, dc == 7, reads=("one128", "f3sq"), writes=("pb5",))
                    vop(lambda e: e.tensor_scalar_mul(out=MRow[:], in0=bank(4)[0:1, :],
                                                      scalar1=1.0 / 1024),
                        reads=("pb4",), writes=("mrow",))
                    vop(lambda e: e.tensor_scalar_mul(out=ERow[:], in0=bank(5)[0:1, :],
                                                      scalar1=1.0 / 1024),
                        reads=("pb5",), writes=("erow",))
                    vop(lambda e: e.tensor_mul(out=TRow[:], in0=MRow[:], in1=MRow[:]),
                        reads=("mrow",), writes=("trow",))
                    vop(lambda e: e.tensor_sub(out=ERow[:], in0=ERow[:], in1=TRow[:]),
                        reads=("erow", "trow"), writes=("erow",))
                    aop(lambda e: e.activation(out=TRow[:], in_=ERow[:], func=AF.Ln,
                                               bias=CVA[0:1, 3:4]),
                        reads=("erow", "epsb"), writes=("trow",))
                    aop(lambda e: e.activation(out=RRow[:], in_=TRow[:], func=AF.Exp,
                                               scale=-0.5),
                        reads=("trow",), writes=("rrow",))
                    vop(lambda e: e.tensor_copy(out=ABR[:], in_=RRow[:]),
                        reads=("rrow",), writes=("abr",))
                    vop(lambda e: e.scalar_tensor_tensor(
                        out=BBR[:], in0=MRow[:], scalar=-1.0, in1=RRow[:],
                        op0=ALU.mult, op1=ALU.mult),
                        reads=("mrow", "rrow"), writes=("bbr",))
                    mm(bank(4), ONE1[:], ABR[:], True, True,
                       reads=("one1", "abr"), writes=("pb4",))
                    vop(lambda e: e.tensor_copy(out=ABC[:], in_=bank(4)),
                        reads=("pb4",), writes=("abc",))
                    mm(bank(5), ONE1[:], BBR[:], True, True,
                       reads=("one1", "bbr"), writes=("pb5",))
                    vop(lambda e: e.tensor_copy(out=BBC[:], in_=bank(5)),
                        reads=("pb5",), writes=("bbc",))
                yield stats

                def xhat():
                    for dc in range(8):
                        vop(lambda e, o0=dc * 512: e.tensor_mul(
                            out=TZ[:], in0=XC[:, o0:o0 + 512], in1=ABC[:]),
                            reads=("p1x", "abc"), writes=("tzf",))
                        vop(lambda e, o0=dc * 512: e.tensor_add(
                            out=XH[:, 4096 + o0: 4096 + o0 + 512], in0=TZ[:], in1=BBC[:]),
                            reads=("tzf", "bbc"), writes=("f3h",))
                yield xhat

                for hch in range(24):
                    def gu(hch=hch):
                        pp = hch % 2
                        dma("sync",
                            WG[:, pp * 1024:(pp + 1) * 1024].rearrange("p (a c) -> p a c", a=8),
                            g_bf.ap()[hch].rearrange("a p c -> p a c"),
                            reads=(), writes=(f"wg{pp}",))
                        dma("sync",
                            WU[:, pp * 1024:(pp + 1) * 1024].rearrange("p (a c) -> p a c", a=8),
                            u_bf.ap()[hch].rearrange("a p c -> p a c"),
                            reads=(), writes=(f"wu{pp}",))
                        for dc in range(8):
                            mm(bank(4), WG[:, pp * 1024 + dc * 128: pp * 1024 + (dc + 1) * 128],
                               XH[:, 4096 + dc * 512: 4096 + (dc + 1) * 512],
                               dc == 0, dc == 7, reads=(f"wg{pp}", "f3h"), writes=("pb4",))
                        for dc in range(8):
                            mm(bank(5), WU[:, pp * 1024 + dc * 128: pp * 1024 + (dc + 1) * 128],
                               XH[:, 4096 + dc * 512: 4096 + (dc + 1) * 512],
                               dc == 0, dc == 7, reads=(f"wu{pp}", "f3h"), writes=("pb5",))
                        aop(lambda e: e.activation(out=KEV[:, 0:512], in_=bank(4),
                                                   func=AF.Silu, scale=QD),
                            reads=("pb4",), writes=("sg",))
                        vop(lambda e, h=hch: e.tensor_mul(
                            out=HT[:, 4096 + h * 512: 4096 + (h + 1) * 512],
                            in0=bank(5), in1=KEV[:, 0:512]),
                            reads=("pb5", "sg"), writes=("ht2",))
                    yield gu

                for dc in range(8):
                    def dn(dc=dc, s=s):
                        pp = dc % 2
                        dma("sync",
                            WD2[:, pp * 3072:(pp + 1) * 3072].rearrange("p (a c) -> p a c", a=24),
                            d_bf.ap()[dc].rearrange("a p c -> p a c"),
                            reads=(), writes=(f"wd{pp}",))
                        for hch in range(24):
                            mm(bank(4 + pp),
                               WD2[:, pp * 3072 + hch * 128: pp * 3072 + (hch + 1) * 128],
                               HT[:, 4096 + hch * 512: 4096 + (hch + 1) * 512],
                               hch == 0, hch == 23, reads=(f"wd{pp}", "ht2"),
                               writes=(f"pb{4 + pp}",))
                        vop(lambda e, d0=dc, b=4 + pp: e.scalar_tensor_tensor(
                            out=TZ[:], in0=bank(b), scalar=0.5 * QD * QD,
                            in1=XC[:, d0 * 512:(d0 + 1) * 512],
                            op0=ALU.mult, op1=ALU.add),
                            reads=(f"pb{4 + pp}", "p1x"), writes=("tzf",))
                        dma("gpsimd",
                            outT.ap()[dc * 128:(dc + 1) * 128, s * 512:(s + 1) * 512],
                            TZ[:],
                            reads=("tzf",), writes=("outd",))
                    yield dn

            fillers = []

            def pump(n=1):
                for _ in range(n):
                    if fillers:
                        fillers.pop(0)()

            it_of = {}
            for i, it in enumerate(ITERS):
                it_of.setdefault(it[0], []).append((i, it[1], it[2]))

            for qi in range(4):
                iters = it_of[12 + qi]
                dma("gpsimd",
                    XB[0:64, :].rearrange("p (f hh c) -> p f hh c", f=8, hh=2),
                    qQ.ap()[qi].rearrange("f (hh p) c -> p f hh c", hh=2),
                    reads=(), writes=("qs", "xbf"))
                for pg in range(8):
                    x = 0
                    pend = []

                    def flush(keep=0):
                        while len(pend) > keep:
                            pend.pop(0)()

                    for ii, (it_idx, ks, kind) in enumerate(iters):
                        pp = ii % 2
                        kb = pp * 1024
                        vb = 2048 + pp * 520
                        dma("gpsimd",
                            HT[0:64, kb:kb + 1024].rearrange("p (hh c) -> p hh c", hh=2),
                            kK.ap()[ks, pg].rearrange("(hh p) c -> p hh c", hh=2),
                            reads=(), writes=(f"kb{pp}",))
                        dma("gpsimd",
                            HT[:, vb:vb + 520].rearrange("p (s c) -> p s c", s=4),
                            kV.ap()[ks][:, :, pg * 130:(pg + 1) * 130],
                            reads=(), writes=(f"vb{pp}",))
                        for hh in range(2):
                            h = 2 * pg + hh
                            for half in range(2):
                                r = x % 2
                                pt = x % 3
                                for s2 in range(2):
                                    sub = half * 2 + s2
                                    mm(PS[r][:, s2 * 512:(s2 + 1) * 512],
                                       HT[:, kb + hh * 512 + sub * 128:
                                          kb + hh * 512 + (sub + 1) * 128],
                                       XB[:, h * 512:(h + 1) * 512],
                                       True, True,
                                       reads=(f"kb{pp}", "qs"),
                                       writes=(f"pb{2 * r + s2}",))
                                aop(lambda e, r0=r, p0=pt, it0=it_idx: e.activation(
                                    out=XQ[:, p0 * 1024:(p0 + 1) * 1024],
                                    in_=PS[r0][:], func=AF.Exp,
                                    bias=BIA[:, it0:it0 + 1], scale=SCALE),
                                    reads=(f"pb{2 * r}", f"pb{2 * r + 1}", "bia"),
                                    writes=(f"pt{pt}",))
                                if kind == 'd':
                                    for s2 in range(2):
                                        sub = half * 2 + s2
                                        vop(lambda e, p0=pt, s0=s2, sb=sub: e.tensor_mul(
                                            out=XQ[:, p0 * 1024 + s0 * 512:
                                                   p0 * 1024 + (s0 + 1) * 512],
                                            in0=XQ[:, p0 * 1024 + s0 * 512:
                                                   p0 * 1024 + (s0 + 1) * 512],
                                            in1=MSK[:, sb * 512:(sb + 1) * 512]),
                                            reads=(f"pt{pt}", "msk"),
                                            writes=(f"pt{pt}",))
                                flush(keep=1)

                                def mk_av(hh0=hh, half0=half, pt0=pt, pp0=pp, vb0=vb,
                                          first=(ii == 0 and half == 0),
                                          last=(ii == len(iters) - 1 and half == 1)):
                                    def go():
                                        for s2 in range(2):
                                            sub = half0 * 2 + s2
                                            mm(bank(6 + hh0)[0:65, :],
                                               HT[:, vb0 + sub * 130 + hh0 * 65:
                                                  vb0 + sub * 130 + (hh0 + 1) * 65],
                                               XQ[:, pt0 * 1024 + s2 * 512:
                                                  pt0 * 1024 + (s2 + 1) * 512],
                                               first and s2 == 0, last and s2 == 1,
                                               reads=(f"vb{pp0}", f"pt{pt0}"),
                                               writes=(f"pb{6 + hh0}",))
                                    return go
                                pend.append(mk_av())
                                x += 1
                        pump(1)
                    flush()
                    for hh in range(2):
                        h = 2 * pg + hh
                        vop(lambda e, b=hh: e.reciprocal(out=RRW[:],
                                                         in_=bank(6 + b)[64:65, :]),
                            reads=(f"pb{6 + hh}",), writes=("rrw",))
                        mm(bank(0)[0:64, :], ONE1[:, 0:64], RRW[:], True, True,
                           reads=("one1", "rrw"), writes=("pb0",))
                        vop(lambda e: e.tensor_copy(out=RBS[0:64, 512:1024],
                                                     in_=bank(0)[0:64, :]),
                            reads=("pb0",), writes=("rbs",))
                        vop(lambda e, b=hh, hd=h: e.tensor_mul(
                            out=XQ[(hd % 2) * 64:(hd % 2) * 64 + 64,
                                   4096 + (hd // 2) * 512: 4096 + (hd // 2 + 1) * 512],
                            in0=bank(6 + b)[0:64, :], in1=RBS[0:64, 512:1024]),
                            reads=(f"pb{6 + hh}", "rbs"), writes=("oall",))
                dma("gpsimd", XM[:, 0:4096].rearrange("p (a c) -> p a c", a=8),
                    xm_d.ap()[:, :, qi * 512:(qi + 1) * 512].rearrange("a p c -> p a c"),
                    reads=(), writes=("xms",))
                for dc in range(8):
                    dma("sync", WP[:, 0:1024].rearrange("p (a c) -> p a c", a=8),
                        p_bf.ap()[dc].rearrange("a p c -> p a c"),
                        reads=(), writes=("wp",))
                    for ech in range(8):
                        mm(bank(2), WP[:, ech * 128:(ech + 1) * 128],
                           XQ[:, 4096 + ech * 512: 4096 + (ech + 1) * 512],
                           ech == 0, ech == 7, reads=("wp", "oall"), writes=("pb2",))
                    vop(lambda e, d0=dc: e.tensor_scalar_add(
                        out=TZ[:], in0=bank(2), scalar1=BPC[:, d0:d0 + 1]),
                        reads=("pb2", f"bp{dc}"), writes=("tzf",))
                    vop(lambda e, d0=dc: e.tensor_add(
                        out=XM[:, 4096 + d0 * 512: 4096 + (d0 + 1) * 512],
                        in0=TZ[:], in1=XM[:, d0 * 512:(d0 + 1) * 512]),
                        reads=("tzf", "xms"), writes=("xaa",))
                dma("gpsimd", xa_d.ap()[:, :, qi * 512:(qi + 1) * 512].rearrange("a p c -> p a c"),
                    XM[:, 4096:8192].rearrange("p (a c) -> p a c", a=8),
                    reads=("xaa",), writes=(f"xad{qi}",))
                fillers.extend(filler_pieces(qi))
            while fillers:
                pump(1)
            if DEV_TAPS:
                dma("gpsimd", taps["t_xa"].ap(), xa_d.ap(),
                    reads=("xad0", "xad1", "xad2", "xad3"), writes=())
                dma("gpsimd", taps["t_k"].ap(), kK.ap()[0], reads=(), writes=())
                dma("gpsimd", taps["t_v"].ap(), kV.ap()[0], reads=(), writes=())
                dma("gpsimd", taps["t_q"].ap(), qQ.ap()[0], reads=(), writes=())
                dma("gpsimd", taps["t_c"].ap(), BIA[:], reads=("bia",), writes=())

        P.op("sync", lambda e: e.nop(),
             reads=tuple(k for k, v in P.last_w.items() if v is not None), writes=())

        _emit(nc, P, ctx)

    nc.compile()
    return nc


def _emit(nc, P, ctx):
    sems = {e: ctx.enter_context(nc.semaphore(f"sem_{e}")) for e in Prog.ENGS}
    lp = nc.allow_low_precision(
        reason="bf16 matmul operands with fp32 PSUM + fp32 elementwise interiors; "
               "bf16 stores are the intended operand precision")
    with lp, nc.Block() as block:
        def section(eng_name):
            def body(eng):
                for waits, emit, inc in P.plan[eng_name]:
                    for (src, val) in waits:
                        eng.wait_ge(sems[src], val)
                    ins = emit(eng)
                    if ins is None:
                        ins = eng.nop()
                    ins.then_inc(sems[eng_name], inc)
            return body
        block.sync(section("sync"))
        block.tensor(section("tensor"))
        block.vector(section("vector"))
        block.scalar(section("scalar"))
        block.gpsimd(section("gpsimd"))


_NC_CACHE = None


def kernel(**inputs):
    global _NC_CACHE
    cores = _host_inputs(**inputs)
    if _NC_CACHE is None:
        _NC_CACHE = build_nc()
    res = run_bass_kernel_spmd(_NC_CACHE, cores, core_ids=list(range(8)))
    return _host_assemble(res.results)


if __name__ == "__main__":
    build_nc()
    print("built ok")
